# version 02 (final)
# baseline (speedup 1.0000x reference)
import numpy as np
import ml_dtypes
import concourse.bass as bass
import concourse.mybir as mybir
from concourse.bass_utils import run_bass_kernel_spmd
from concourse.alu_op_type import AluOpType
from contextlib import ExitStack

F32 = mybir.dt.float32
F32R = mybir.dt.float32r
BF16 = mybir.dt.bfloat16
U32 = mybir.dt.uint32
AF = mybir.ActivationFunctionType
OP = AluOpType
AXX = mybir.AxisListType.X

B, S, DM = 4, 8192, 2048
DI, DIH = 4096, 2048
DE = 8192
T = 512
NCH = S // T
G = 16
SH = S // 2
EPS_LN = 1e-5
EPS_Q = 1e-8
RG_PAIRS = [[0, 1], [2, 3], [4, 5], [6, 7]]


class Sem:
    def __init__(self, ctx, nc, name):
        self.h = ctx.enter_context(nc.semaphore(name))
        self.n = 0

    def inc(self, k=1):
        self.n += k
        return self.n


class Emitter:
    ENGS = ("sync", "scalar", "vector", "tensor", "gpsimd")

    def __init__(self):
        self.ops = {e: [] for e in self.ENGS}

    def on(self, eng):
        def deco(fn):
            self.ops[eng].append(fn)
        return deco

    def replay(self, block):
        ops = self.ops

        @block.sync
        def _(e):
            for f in ops["sync"]:
                f(e)

        @block.scalar
        def _(e):
            for f in ops["scalar"]:
                f(e)

        @block.vector
        def _(e):
            for f in ops["vector"]:
                f(e)

        @block.tensor
        def _(e):
            for f in ops["tensor"]:
                f(e)

        @block.gpsimd
        def _(e):
            for f in ops["gpsimd"]:
                f(e)


class Bld:
    def __init__(self, stages=3, dbg=0, nch=NCH, cc=True):
        self.stages = stages
        self.dbg = dbg
        self.nch = nch
        self.cc = cc

    def build(self):
        nc = bass.Bass("TRN2", target_bir_lowering=False)
        self.nc = nc

        I = {}
        for nm, shape, dt in [
            ("x_in", [S, DM], F32), ("xc_in", [SH, DM], F32),
            ("winx", [DIH, DM], F32), ("winz", [DIH, DM], F32),
            ("convw", [128, 64], F32), ("convb", [128, 16], F32),
            ("wxt", [DIH, 33], F32), ("apair", [128, 256], F32),
            ("dtb_in", [128, 32], F32), ("dvec", [128, 16], F32),
            ("outw", [DM, DI], F32),
            ("wup", [DE, DM], F32), ("wgate", [DE, DM], F32), ("wdown", [DM, DE], F32),
            ("identb", [128, 128], BF16),
            ("maskrep", [128, 16 * 128], BF16), ("masky", [128, 16 * 128], F32R),
            ("ones_col", [128, 1], F32), ("ones_row", [1, 128], F32),
            ("choff", [1, 8], U32),
        ]:
            I[nm] = nc.dram_tensor(nm, shape, dt, kind="ExternalInput").ap()
        self.I = I
        self.y_out = nc.dram_tensor("y_out", [SH, DM], F32, kind="ExternalOutput").ap()
        DBG = {}
        if self.dbg >= 1:
            DBG["xconv"] = nc.dram_tensor("dbg_xconv", [DIH, S], F32, kind="ExternalOutput").ap()
            DBG["p"] = nc.dram_tensor("dbg_p", [33, S], F32, kind="ExternalOutput").ap()
        if self.dbg >= 2:
            DBG["ymix"] = nc.dram_tensor("dbg_ymix", [DIH, S], F32, kind="ExternalOutput").ap()
        if self.dbg >= 4:
            DBG["xssm"] = nc.dram_tensor("dbg_xssm", [DIH, S], F32, kind="ExternalOutput").ap()
            DBG["ln1t4"] = nc.dram_tensor("dbg_ln1t4", [128, G * T], F32, kind="ExternalOutput").ap() if self.dbg == 4 else None
        if self.dbg >= 3:
            DBG["x2"] = nc.dram_tensor("dbg_x2", [SH, DM], F32, kind="ExternalOutput").ap()
            DBG["wq"] = nc.dram_tensor("dbg_wq", [128, DE], F32, kind="ExternalOutput").ap()
            DBG["outf"] = nc.dram_tensor("dbg_outf", [SH, DM], F32, kind="ExternalOutput").ap()
            DBG["ymo"] = nc.dram_tensor("dbg_ymo", [DI, T], BF16, kind="ExternalOutput").ap()
            DBG["yma"] = nc.dram_tensor("dbg_yma", [128, 32 * 256], F32, kind="ExternalOutput").ap()
            DBG["csc"] = nc.dram_tensor("dbg_csc", [128, 16], F32, kind="ExternalOutput").ap()
            DBG["xcs"] = nc.dram_tensor("dbg_xcs", [128, DM], F32, kind="ExternalOutput").ap()
            DBG["onorm"] = nc.dram_tensor("dbg_onorm", [128, DM], F32, kind="ExternalOutput").ap()
            DBG["hid"] = nc.dram_tensor("dbg_hid", [64 * T], F32, kind="ExternalOutput").ap()
        self.DBG = DBG

        D = {}
        for nm, shape, dt in [
            ("xconv_sp", [NCH, 128, G * T], F32), ("ln1T_sp", [NCH, 128, G * T], BF16),
            ("p_in", [33, S], F32), ("p_out", [33, S], F32),
            ("rep_sp", [NCH, 17, T], F32),
            ("winxT_s", [DM, DIH], BF16), ("winzT_s", [DM, DIH], BF16),
            ("outwT_s", [DI, DM], BF16),
            ("ymix_in", [NCH, DIH, T], BF16), ("ymix_out", [NCH, DI, T], BF16),
            ("x2_sp", [8, T, DM], F32),
            ("wupT_s", [DM, DE], BF16), ("wgateT_s", [DM, DE], BF16),
            ("wdownT_s", [DE, DM], BF16),
        ]:
            D[nm] = nc.dram_tensor(nm, shape, dt)
        self.D = D

        with ExitStack() as ps:
            self.e = ps.enter_context
            e = self.e
            self.sm = {}
            self.dma_sems = {"sync": [], "scalar": [], "gpsimd": []}
            def mk(name, issuer=None):
                s = Sem(ps, nc, name)
                self.sm[name] = s
                if issuer:
                    self.dma_sems[issuer].append(s)
                return s
            self.mk = mk
            self.PE = mk("pe"); self.VE = mk("ve"); self.SE = mk("se")
            self.GE = mk("ge"); self.CC = mk("cc"); self.BAR = mk("bar")
            self.s_const = mk("s_const", "sync")
            self.s_strip = mk("s_strip", "gpsimd")
            self.s_spill = mk("s_spill", "gpsimd")
            self.s_ccp = mk("s_ccp", "gpsimd")
            self.s_ccy = mk("s_ccy", "gpsimd")
            self.s_xsl = [mk("s_xsl0", "scalar"), mk("s_xsl1", "scalar")]
            self.s_winxq = [mk("s_winxq0", "sync"), mk("s_winxq1", "sync")]
            self.s_prepw = [mk("s_prepw0", "sync"), mk("s_prepw1", "sync")]
            self.s_xcr = mk("s_xcr", "sync")
            self.s_ln1r = mk("s_ln1r", "sync")
            self.s_winz = mk("s_winz", "sync")
            self.s_pch2 = mk("s_pch2", "scalar")
            self.s_repout = mk("s_repout", "scalar")
            self.s_dtrep = [mk("s_dtrep0", "scalar"), mk("s_dtrep1", "scalar")]
            self.s_dtbc = [mk("s_dtbc0", "scalar"), mk("s_dtbc1", "scalar")]
            self.s_ymixsp = mk("s_ymixsp", "gpsimd")
            self.s_ymagg = mk("s_ymagg", "sync")
            self.s_outw = mk("s_outw", "sync")
            self.s_wupg = mk("s_wupg", "sync")
            self.s_dwn = mk("s_dwn", "sync")
            self.s_xc2 = mk("s_xc2", "scalar")
            self.s_x2sp = mk("s_x2sp", "gpsimd")
            self.s_yout = mk("s_yout", "gpsimd")

            self.PSF = [e(nc.psum_tensor(f"psf{i}", [128, 512], F32)) for i in range(6)]
            self.PSB = [e(nc.psum_tensor(f"psb{i}", [128, 1024], BF16)) for i in range(2)]
            self.psf_free = [(self.VE, 0)] * 6
            self.psb_free = [(self.VE, 0)] * 2

            t = lambda nm, sh, dt: e(nc.sbuf_tensor(nm, sh, dt))
            self.ident_sb = t("ident_sb", [128, 128], BF16)
            self.mrep_sb = t("mrep_sb", [128, 16 * 128], BF16)
            self.my_sb = t("my_sb", [128, 16 * 128], F32R)
            self.onesc = t("onesc", [128, 1], F32)
            self.onesr = t("onesr", [1, 128], F32)
            self.convw_sb = t("convw_sb", [128, 64], F32)
            self.convb_sb = t("convb_sb", [128, 16], F32)
            self.wxt_sb = t("wxt_sb", [128, 16 * 33], F32)
            self.negA = t("negA", [128, 256], F32)
            self.dvec_sb = t("dvec_sb", [128, 16], F32)
            self.hcar = t("hcar", [128, 256], F32)
            self.dtm_sb = t("dtm_sb", [1, 1], F32)
            self.stats = t("stats", [128, 224], F32)
            self.qsc = t("qsc", [128, 16], F32)
            self.choff_sb = t("choff_sb", [1, 8], U32)

            with ExitStack() as preps:
                ep = preps.enter_context
                self.prep_w = ep(nc.sbuf_tensor("prep_w", [128, 2 * 2048], F32))
                self.prep_b = ep(nc.sbuf_tensor("prep_b", [128, 2048], BF16))
                self.prep_v = ep(nc.sbuf_tensor("prep_v", [128, 2048], BF16))
                self.prep_o = ep(nc.sbuf_tensor("prep_o", [128, 2048], BF16))
                self.prep_st = {"i": 0, "free": [0, 0], "free_sem": [self.VE, self.VE],
                                "tr": 0, "strip": 0}
                self.bg = []

                with ExitStack() as ascope:
                    self.em = Emitter()
                    self.block0(ascope.enter_context)
                    with nc.Block("b0") as blk:
                        self.em.replay(blk)

                with ExitStack() as bscope:
                    self.em = Emitter()
                    self.block1(bscope.enter_context)
                    with nc.Block("b1") as blk:
                        self.em.replay(blk)

            with ExitStack() as cscope:
                self.em = Emitter()
                self.block2(cscope.enter_context)
                with nc.Block("b2") as blk:
                    self.em.replay(blk)

        return nc

    def barrier(self, phase):
        em = self.em
        BAR = self.BAR

        @em.on("sync")
        def _(e, wl=tuple((s.h, s.n) for s in self.dma_sems["sync"] if s.n)):
            for h, v in wl:
                e.wait_ge(h, v)
            e.nop().then_inc(BAR.h, 1)

        @em.on("scalar")
        def _(e, wl=tuple((s.h, s.n) for s in self.dma_sems["scalar"] if s.n)):
            for h, v in wl:
                e.wait_ge(h, v)
            e.copy(self.dtm_sb[0:1, 0:1], self.dtm_sb[0:1, 0:1]).then_inc(BAR.h, 1)

        @em.on("vector")
        def _(e):
            e.tensor_copy(self.stats[0:1, 220:221], self.stats[0:1, 220:221]).then_inc(BAR.h, 1)

        @em.on("tensor")
        def _(e):
            e.nop().then_inc(BAR.h, 1)

        @em.on("gpsimd")
        def _(e, wl=tuple((s.h, s.n) for s in self.dma_sems["gpsimd"] if s.n),
              b=self.CC.n):
            for h, v in wl:
                e.wait_ge(h, v)
            if b:
                e.wait_ge(self.CC.h, b)
            e.nop().then_inc(BAR.h, 1)
        BAR.inc(5)

    def barrier_entry(self, val):
        em = self.em
        for eng in Emitter.ENGS:
            @em.on(eng)
            def _(e, val=val):
                e.wait_ge(self.BAR.h, val)

    def prep_tile(self, src_ap, dst, dst_off, dst_L, quant=None):
        em, st = self.em, self.prep_st
        SE, VE, GE, PE = self.SE, self.VE, self.GE, self.PE
        buf = st["i"] % 2
        st["i"] += 1
        cs = slice(buf * 2048, (buf + 1) * 2048)

        sw = self.s_prepw[buf]

        @em.on("sync")
        def _(e, fs=st["free_sem"][buf], fv=st["free"][buf], src=src_ap, sw=sw):
            e.wait_ge(fs.h, fv)
            e.dma_start(self.prep_w[:, cs], src).then_inc(sw.h, 16)
        lw = sw.inc(16)

        if quant is None:
            @em.on("vector")
            def _(v, lw=lw, wtr=st["tr"], sw=sw):
                v.wait_ge(sw.h, lw)
                v.wait_ge(PE.h, wtr)
                v.tensor_copy(self.prep_b[:], self.prep_w[:, cs]).then_inc(VE.h, 1)
            st["free"][buf] = VE.inc()
            st["free_sem"][buf] = VE
            cast_sem, cast_val = VE, st["free"][buf]
        else:
            thr_col, nthr_col, rdy = quant

            @em.on("vector")
            def _(v, lw=lw, wtr=st["tr"], tc=thr_col, ntc=nthr_col, rdy=rdy, sw=sw):
                v.wait_ge(sw.h, lw)
                v.wait_ge(PE.h, wtr)
                v.wait_ge(rdy[0].h, rdy[1])
                v.tensor_scalar(self.prep_v[:], self.prep_w[:, cs], ntc, None, OP.is_lt)
                v.scalar_tensor_tensor(self.prep_b[:], self.prep_w[:, cs], tc,
                                       self.prep_v[:], OP.is_gt, OP.subtract).then_inc(VE.h, 1)
            st["free"][buf] = VE.inc()
            st["free_sem"][buf] = VE
            cast_sem, cast_val = VE, st["free"][buf]

        for half in range(2):
            pb = half
            pfree = self.psb_free[pb]

            @em.on("tensor")
            def _(t, half=half, pb=pb, cv=cast_val, cse=cast_sem, pfree=pfree, stv=st["strip"]):
                t.wait_ge(cse.h, cv)
                t.wait_ge(pfree[0].h, pfree[1])
                t.wait_ge(self.s_strip.h, stv)
                for ci in range(8):
                    cc = half * 8 + ci
                    mm = t.transpose(self.PSB[pb][:, ci * 128:(ci + 1) * 128],
                                     self.prep_b[:, cc * 128:(cc + 1) * 128], self.ident_sb[:])
                mm.then_inc(PE.h, 1)
            pe_tr = PE.inc()

            @em.on("scalar")
            def _(s, half=half, pb=pb, w=pe_tr):
                s.wait_ge(PE.h, w)
                s.copy(self.prep_o[:, half * 1024:(half + 1) * 1024], self.PSB[pb][:, :]).then_inc(SE.h, 1)
            self.psb_free[pb] = (SE, SE.inc())
        st["tr"] = PE.n
        se_strip = SE.n

        @em.on("gpsimd")
        def _(g, w=se_strip, dst=dst, off=dst_off, L=dst_L):
            g.wait_ge(SE.h, w)
            g.dma_start(bass.AP(dst, off, [[L, 128], [128 * L, 16], [1, 128]]),
                        self.prep_o[:]).then_inc(self.s_strip.h, 16)
        st["strip"] = self.s_strip.inc(16)

    def abs_tile(self, src_ap, stat_col):
        em, st = self.em, self.prep_st
        SE, PE = self.SE, self.PE
        buf = st["i"] % 2
        st["i"] += 1
        cs = slice(buf * 2048, (buf + 1) * 2048)

        sw = self.s_prepw[buf]

        @em.on("sync")
        def _(e, fs=st["free_sem"][buf], fv=st["free"][buf], src=src_ap, sw=sw):
            e.wait_ge(fs.h, fv)
            e.dma_start(self.prep_w[:, cs], src).then_inc(sw.h, 16)
        lw = sw.inc(16)

        @em.on("scalar")
        def _(s, lw=lw, col=stat_col, sw=sw):
            s.wait_ge(sw.h, lw)
            s.activation(self.prep_dump[:, :], self.prep_w[:, cs],
                         AF.Abs, bias=0.0, scale=1.0,
                         accum_out=self.stats[:, col:col + 1]).then_inc(SE.h, 1)
        st["free"][buf] = SE.inc()
        st["free_sem"][buf] = SE

    def drain_bg(self, n):
        for _ in range(min(n, len(self.bg))):
            self.bg.pop(0)()

    def block0(self, e):
        nc, em, I, D = self.nc, self.em, self.I, self.D
        SE, VE, PE, GE = self.SE, self.VE, self.PE, self.GE
        dbg = self.dbg

        @em.on("sync")
        def _(eng):
            for dst, src in [
                (self.ident_sb[:], I["identb"][:, :]), (self.mrep_sb[:], I["maskrep"][:, :]),
                (self.my_sb[:], I["masky"][:, :]), (self.onesc[:], I["ones_col"][:, :]),
                (self.onesr[:], I["ones_row"][:, :]), (self.convw_sb[:], I["convw"][:, :]),
                (self.convb_sb[:], I["convb"][:, :]),
                (self.wxt_sb[:], bass.AP(I["wxt"].tensor, 0, [[33, 128], [128 * 33, 16], [1, 33]])),
                (self.negA[:], I["apair"][:, :]), (self.dvec_sb[:], I["dvec"][:, :]),
                (self.stats[:, 192:224], I["dtb_in"][:, :]),
                (self.choff_sb[0:1, 0:8], I["choff"][0:1, 0:8]),
            ]:
                eng.dma_start(dst, src).then_inc(self.s_const.h, 16)
        LD0 = self.s_const.inc(16 * 12)

        @em.on("scalar")
        def _(s, w=LD0):
            s.wait_ge(self.s_const.h, w)
            s.activation(self.negA[:], self.negA[:], AF.Exp, bias=0.0, scale=1.0).then_inc(SE.h, 1)
        se_expa = SE.inc()

        @em.on("vector")
        def _(v, w=LD0, w2=se_expa):
            v.wait_ge(self.s_const.h, w)
            v.memset(self.hcar[:], 0.0)
            v.tensor_reduce(self.stats[:, 216:217], self.stats[:, 192:224], AXX, OP.add)
            v.wait_ge(SE.h, w2)
            v.tensor_scalar(self.negA[:], self.negA[:], -1.0, None, OP.mult).then_inc(VE.h, 1)
        ve_nega = VE.inc()

        @em.on("tensor")
        def _(t, w=ve_nega):
            t.wait_ge(VE.h, w)
            t.matmul(self.PSF[0][0:1, 0:1], self.onesc[:], self.stats[:, 216:217],
                     start=True, stop=True).then_inc(PE.h, 1)
        pe_dtm = PE.inc()

        @em.on("scalar")
        def _(s, w=pe_dtm):
            s.wait_ge(PE.h, w)
            s.activation(self.dtm_sb[:], self.PSF[0][0:1, 0:1], AF.Copy,
                         bias=0.0, scale=1.0 / 4096.0).then_inc(SE.h, 1)
        self.psf_free[0] = (SE, SE.inc())

        for rt in range(16):
            self.prep_tile(I["winx"][rt * 128:(rt + 1) * 128, :], D["winxT_s"], rt * 128, DIH)
        self.SPG_WINX = self.s_strip.n

        for rt in range(16):
            self.bg.append(lambda rt=rt: self.prep_tile(
                I["winz"][rt * 128:(rt + 1) * 128, :], D["winzT_s"], rt * 128, DIH))
        for rt in range(16):
            for qc in range(2):
                self.bg.append(lambda rt=rt, qc=qc: self.prep_tile(
                    I["outw"][rt * 128:(rt + 1) * 128, qc * 2048:(qc + 1) * 2048],
                    D["outwT_s"], qc * 2048 * DM + rt * 128, DM))
        if self.stages >= 3:
            for i in range(64):
                self.bg.append(lambda i=i: self.abs_tile(I["wup"][i * 128:(i + 1) * 128, :], i))
            for i in range(64):
                self.bg.append(lambda i=i: self.abs_tile(I["wgate"][i * 128:(i + 1) * 128, :], 64 + i))
            for i in range(64):
                rt, qc = i // 4, i % 4
                self.bg.append(lambda rt=rt, qc=qc, i=i: self.abs_tile(
                    I["wdown"][rt * 128:(rt + 1) * 128, qc * 2048:(qc + 1) * 2048], 128 + i))

        t = lambda nm, sh, dt: e(nc.sbuf_tensor(nm, sh, dt))
        self.prep_dump = t("prep_dump", [128, 2048], F32)
        self.p_ch = t("p_ch", [33, T], F32)
        x_sl = t("x_sl", [128, 2 * DM], F32)
        ln1_sl = t("ln1_sl", [128, 2 * DM], BF16)
        ln1T = t("ln1T", [128, 2 * G * T], BF16)
        xssm = t("xssm", [128, G * (T + 3)], F32)
        xconv = t("xconv", [128, G * T], F32)
        winx_bb = t("winx_bb", [128, 2 * G * 256], BF16)
        asc = t("asc", [128, 24], F32)
        TH3 = T + 3

        conv_done = [0] * G
        spill_done = 0
        pep_done = 0
        norm_done = [0, 0]
        tr_done = [0, 0]
        sq_free = [0, 0]
        xq_mm = [0, 0]
        ln1t_users_pe = 0
        ln1t_users_sp = 0

        for k in range(self.nch):
            for m in range(4):
                sl = m % 2

                sx = self.s_xsl[sl]

                @em.on("scalar")
                def _(eng, k=k, m=m, sl=sl, w=norm_done[sl], sx=sx):
                    eng.wait_ge(VE.h, w)
                    eng.dma_start(x_sl[:, sl * DM:(sl + 1) * DM],
                                  I["x_in"][k * T + m * 128:k * T + (m + 1) * 128, :]).then_inc(sx.h, 16)
                lx = sx.inc(16)

                @em.on("vector")
                def _(v, sl=sl, lx=lx, sx=sx):
                    v.wait_ge(sx.h, lx)
                    v.tensor_reduce(asc[:, sl:sl + 1], x_sl[:, sl * DM:(sl + 1) * DM],
                                    AXX, OP.add).then_inc(VE.h, 1)
                ve_sum = VE.inc()

                @em.on("scalar")
                def _(s, sl=sl, lx=lx, w=sq_free[sl], sx=sx):
                    s.wait_ge(sx.h, lx)
                    s.wait_ge(VE.h, w)
                    s.activation(self.prep_dump[:, :], x_sl[:, sl * DM:(sl + 1) * DM],
                                 AF.Square, bias=0.0, scale=1.0,
                                 accum_out=asc[:, 2 + sl:3 + sl]).then_inc(SE.h, 1)
                se_sq = SE.inc()

                @em.on("vector")
                def _(v, sl=sl, w1=ve_sum, w2=se_sq, base=VE.n):
                    v.wait_ge(SE.h, w2)
                    v.wait_ge(VE.h, w1)
                    c = sl
                    v.tensor_scalar(asc[:, c + 4:c + 5], asc[:, c:c + 1], 1.0 / DM, None, OP.mult).then_inc(VE.h, 1)
                    v.wait_ge(VE.h, base + 1)
                    v.tensor_tensor(asc[:, c + 6:c + 7], asc[:, c + 4:c + 5], asc[:, c + 4:c + 5], OP.mult)
                    v.tensor_scalar(asc[:, c + 8:c + 9], asc[:, c + 2:c + 3], 1.0 / DM, None, OP.mult).then_inc(VE.h, 1)
                    v.wait_ge(VE.h, base + 2)
                    v.tensor_tensor(asc[:, c + 8:c + 9], asc[:, c + 8:c + 9], asc[:, c + 6:c + 7],
                                    OP.subtract).then_inc(VE.h, 1)
                    v.wait_ge(VE.h, base + 3)
                    v.tensor_scalar(asc[:, c + 8:c + 9], asc[:, c + 8:c + 9], EPS_LN, None,
                                    OP.add).then_inc(VE.h, 1)
                VE.inc(3)
                ve_var = VE.inc()
                sq_free[sl] = ve_var

                @em.on("scalar")
                def _(s, sl=sl, w=ve_var, base=SE.n):
                    s.wait_ge(VE.h, w)
                    c = sl
                    s.activation(asc[:, c + 8:c + 9], asc[:, c + 8:c + 9], AF.Ln, bias=0.0,
                                 scale=1.0).then_inc(SE.h, 1)
                    s.wait_ge(SE.h, base + 1)
                    s.activation(asc[:, c + 8:c + 9], asc[:, c + 8:c + 9], AF.Exp,
                                 bias=0.0, scale=-0.5).then_inc(SE.h, 1)
                SE.inc()
                se_rstd = SE.inc()

                @em.on("vector")
                def _(v, sl=sl, w=se_rstd, w2=tr_done[sl]):
                    v.wait_ge(SE.h, w)
                    v.wait_ge(PE.h, w2)
                    c = sl
                    v.tensor_scalar(ln1_sl[:, sl * DM:(sl + 1) * DM], x_sl[:, sl * DM:(sl + 1) * DM],
                                    asc[:, c + 4:c + 5], asc[:, c + 8:c + 9],
                                    OP.subtract, OP.mult).then_inc(VE.h, 1)
                norm_done[sl] = VE.inc()

                for half in range(2):
                    pb = half
                    pfree = self.psb_free[pb]

                    @em.on("tensor")
                    def _(tt, sl=sl, half=half, pb=pb, w=norm_done[sl], pfree=pfree):
                        tt.wait_ge(VE.h, w)
                        tt.wait_ge(pfree[0].h, pfree[1])
                        for ci in range(8):
                            dg = half * 8 + ci
                            mm = tt.transpose(self.PSB[pb][:, ci * 128:(ci + 1) * 128],
                                              ln1_sl[:, sl * DM + dg * 128: sl * DM + (dg + 1) * 128],
                                              self.ident_sb[:])
                        mm.then_inc(PE.h, 1)
                    tr_done[sl] = PE.inc()

                    @em.on("scalar")
                    def _(s, half=half, pb=pb, m=m, k=k, w=tr_done[sl], w2=ln1t_users_pe, w3=ln1t_users_sp):
                        s.wait_ge(PE.h, w)
                        s.wait_ge(PE.h, w2)
                        s.wait_ge(self.s_spill.h, w3)
                        s.copy(bass.AP(ln1T, (k % 2) * G * T + half * 8 * T + m * 128,
                                       [[2 * G * T, 128], [T, 8], [1, 128]]),
                               bass.AP(self.PSB[pb], 0, [[1024, 128], [128, 8], [1, 128]])).then_inc(SE.h, 1)
                    self.psb_free[pb] = (SE, SE.inc())
            SE_LN1T = SE.n

            @em.on("gpsimd")
            def _(g, k=k, w=SE_LN1T):
                g.wait_ge(SE.h, w)
                g.dma_start(D["ln1T_sp"].ap()[k, :, :], ln1T[:, (k % 2) * G * T:(k % 2 + 1) * G * T]).then_inc(self.s_spill.h, 16)
            ln1t_users_sp = self.s_spill.inc(16)

            for qt in range(8):
                qb = qt % 2

                swq = self.s_winxq[qb]

                @em.on("sync")
                def _(eng, qt=qt, qb=qb, w=xq_mm[qb], wz=self.SPG_WINX, swq=swq):
                    eng.wait_ge(PE.h, w)
                    eng.wait_ge(self.s_strip.h, wz)
                    eng.dma_start(winx_bb[:, qb * G * 256:(qb + 1) * G * 256],
                                  bass.AP(D["winxT_s"], qt * 256, [[DIH, 128], [128 * DIH, 16], [1, 256]])
                                  ).then_inc(swq.h, 16)
                lwq = swq.inc(16)

                for cgl in range(2):
                    cg = qt * 2 + cgl
                    pf = cg % 2
                    pfree = self.psf_free[pf]

                    @em.on("tensor")
                    def _(tt, cg=cg, cgl=cgl, qb=qb, pf=pf, k=k, w1=SE_LN1T, w2=lwq, pfree=pfree, swq=swq):
                        tt.wait_ge(SE.h, w1)
                        tt.wait_ge(swq.h, w2)
                        tt.wait_ge(pfree[0].h, pfree[1])
                        for dmg in range(16):
                            base = qb * G * 256 + dmg * 256 + cgl * 128
                            mm = tt.matmul(self.PSF[pf][:, :], winx_bb[:, base:base + 128],
                                           ln1T[:, (k % 2) * G * T + dmg * T:(k % 2) * G * T + (dmg + 1) * T],
                                           start=(dmg == 0), stop=(dmg == 15))
                        mm.then_inc(PE.h, 1)
                    pe_xz = PE.inc()
                    xq_mm[qb] = pe_xz

                    @em.on("scalar")
                    def _(s, cg=cg, pf=pf, w=pe_xz, w2=conv_done[cg]):
                        s.wait_ge(PE.h, w)
                        s.wait_ge(VE.h, w2)
                        s.copy(xssm[:, cg * TH3 + 3: cg * TH3 + 3 + T], self.PSF[pf][:, :]).then_inc(SE.h, 1)
                    se_xz = SE.inc()
                    self.psf_free[pf] = (SE, se_xz)

                    @em.on("vector")
                    def _(v, k=k, cg=cg, w=se_xz, w2=spill_done, w3=pep_done):
                        v.wait_ge(SE.h, w)
                        v.wait_ge(self.s_spill.h, w2)
                        v.wait_ge(PE.h, w3)
                        off = cg * TH3
                        if k == 0:
                            v.memset(xssm[:, off:off + 3], 0.0)
                        v.tensor_scalar(xconv[:, cg * T:(cg + 1) * T], xssm[:, off:off + T],
                                        self.convw_sb[:, cg * 4: cg * 4 + 1], None, OP.mult)
                        for kk in range(1, 4):
                            v.scalar_tensor_tensor(xconv[:, cg * T:(cg + 1) * T],
                                                   xssm[:, off + kk: off + kk + T],
                                                   self.convw_sb[:, cg * 4 + kk: cg * 4 + kk + 1],
                                                   xconv[:, cg * T:(cg + 1) * T], OP.mult, OP.add)
                        v.tensor_copy(xssm[:, off: off + 3], xssm[:, off + T: off + T + 3]).then_inc(VE.h, 1)
                    ve_conv = VE.inc()
                    conv_done[cg] = ve_conv

                    @em.on("scalar")
                    def _(s, cg=cg, w=ve_conv):
                        s.wait_ge(VE.h, w)
                        s.activation(xconv[:, cg * T:(cg + 1) * T], xconv[:, cg * T:(cg + 1) * T],
                                     AF.Silu, bias=self.convb_sb[:, cg:cg + 1], scale=1.0).then_inc(SE.h, 1)
                    SE.inc()
            SE_XCONV = SE.n
            ln1t_users_pe = xq_mm[1 - (3 % 2)] if False else max(xq_mm)

            pfree = self.psf_free[2]

            @em.on("tensor")
            def _(tt, w=SE_XCONV, pfree=pfree):
                tt.wait_ge(SE.h, w)
                tt.wait_ge(pfree[0].h, pfree[1])
                for g in range(G):
                    mm = tt.matmul(self.PSF[2][0:33, :], self.wxt_sb[:, g * 33:(g + 1) * 33],
                                   xconv[:, g * T:(g + 1) * T], start=(g == 0), stop=(g == 15))
                mm.then_inc(PE.h, 1)
            pep_done = PE.inc()

            @em.on("scalar")
            def _(s, w=pep_done):
                s.wait_ge(PE.h, w)
                s.copy(self.p_ch[0:33, :], self.PSF[2][0:33, :]).then_inc(SE.h, 1)
            se_p = SE.inc()
            self.psf_free[2] = (SE, se_p)

            @em.on("gpsimd")
            def _(g, k=k, w=se_p):
                g.wait_ge(SE.h, w)
                g.dma_start(bass.AP(D["p_in"], k * T, [[S, 33], [1, T]]),
                            self.p_ch[0:33, :]).then_inc(self.s_spill.h, 16)
            self.s_spill.inc(16)

            @em.on("gpsimd")
            def _(g, k=k, w=SE_XCONV):
                g.wait_ge(SE.h, w)
                g.dma_start(D["xconv_sp"].ap()[k, :, :], xconv[:]).then_inc(self.s_spill.h, 16)
            spill_done = self.s_spill.inc(16)

            if self.dbg >= 4:
                @em.on("gpsimd")
                def _(g, k=k, w=SE_XCONV):
                    g.wait_ge(SE.h, w)
                    g.dma_start(bass.AP(self.DBG["xssm"].tensor, k * T, [[S, 128], [128 * S, 16], [1, T]]),
                                bass.AP(xssm, 3, [[G * (T + 3), 128], [T + 3, 16], [1, T]])
                                ).then_inc(self.s_spill.h, 16)
                spill_done = self.s_spill.inc(16)
                if k == 4 and self.dbg == 4:
                    @em.on("gpsimd")
                    def _(g, w=SE_XCONV):
                        g.dma_start(self.DBG["ln1t4"][:, :], ln1T[:, 0:G * T]).then_inc(self.s_spill.h, 16)
                    spill_done = self.s_spill.inc(16)

            if dbg >= 1:
                @em.on("gpsimd")
                def _(g, k=k, w=SE_XCONV):
                    g.wait_ge(SE.h, w)
                    g.dma_start(bass.AP(self.DBG["xconv"].tensor, k * T, [[S, 128], [128 * S, 16], [1, T]]),
                                xconv[:]).then_inc(self.s_spill.h, 16)
                spill_done = self.s_spill.inc(16)

            self.drain_bg(0 if self.dbg == 5 else 15)
        self.drain_bg(10000)
        SPG_A = self.s_spill.n

        if self.cc:
            @em.on("gpsimd")
            def _(g, w=SPG_A):
                g.wait_ge(self.s_spill.h, w)
                g.collective_compute("AllReduce", OP.add, replica_groups=RG_PAIRS,
                                     ins=[D["p_in"].ap().opt()],
                                     outs=[D["p_out"].ap().opt()]).then_inc(self.s_ccp.h, 1)
            self.cc_p = self.s_ccp.inc()
        else:
            @em.on("gpsimd")
            def _(g, w=SPG_A):
                g.wait_ge(self.s_spill.h, w)
                g.dma_start(D["p_out"].ap()[:, 0:self.nch * T], D["p_in"].ap()[:, 0:self.nch * T]).then_inc(self.s_ccp.h, 16)
            self.cc_p = self.s_ccp.inc(16)

        if dbg >= 1:
            @em.on("gpsimd")
            def _(g, w=self.cc_p):
                g.wait_ge(self.s_ccp.h, w)
                g.dma_start(self.DBG["p"][:, 0:self.nch * T], D["p_out"].ap()[:, 0:self.nch * T]).then_inc(self.s_spill.h, 16)
            self.s_spill.inc(16)

        self.barrier(1)

    def block1(self, e):
        nc, em, I, D = self.nc, self.em, self.I, self.D
        SE, VE, PE, GE = self.SE, self.VE, self.PE, self.GE
        self.barrier_entry(5)
        stages, dbg = self.stages, self.dbg

        def scale_reduce(stat0, qcol):
            @em.on("vector")
            def _(v, w=SE.n, stat0=stat0):
                v.wait_ge(SE.h, w)
                v.tensor_reduce(self.stats[:, 216:217], self.stats[:, stat0:stat0 + 64],
                                AXX, OP.add).then_inc(VE.h, 1)
            ve_r = VE.inc()
            pfree = self.psf_free[2]

            @em.on("tensor")
            def _(t, w=ve_r, pfree=pfree):
                t.wait_ge(VE.h, w)
                t.wait_ge(pfree[0].h, pfree[1])
                t.matmul(self.PSF[2][0:1, 0:1], self.onesc[:], self.stats[:, 216:217],
                         start=True, stop=True).then_inc(PE.h, 1)
            pe_r = PE.inc()

            @em.on("scalar")
            def _(s, w=pe_r):
                s.wait_ge(PE.h, w)
                s.activation(self.stats[0:1, 217:218], self.PSF[2][0:1, 0:1], AF.Copy,
                             bias=EPS_Q, scale=1.0 / (DE * DM)).then_inc(SE.h, 1)
            se_r = SE.inc()

            @em.on("vector")
            def _(v, w=se_r):
                v.wait_ge(SE.h, w)
                v.tensor_scalar(self.stats[0:1, 218:219], self.stats[0:1, 217:218],
                                0.5, None, OP.mult).then_inc(VE.h, 1)
            ve_t = VE.inc()

            @em.on("tensor")
            def _(t, w=ve_t):
                t.wait_ge(VE.h, w)
                t.matmul(self.PSF[2][:, 0:1], self.onesr[:], self.stats[0:1, 217:218],
                         start=True, stop=True)
                t.matmul(self.PSF[2][:, 1:2], self.onesr[:], self.stats[0:1, 218:219],
                         start=True, stop=True).then_inc(PE.h, 1)
            pe_b = PE.inc()

            @em.on("scalar")
            def _(s, w=pe_b, qcol=qcol):
                s.wait_ge(PE.h, w)
                s.copy(self.qsc[:, qcol:qcol + 1], self.PSF[2][:, 0:1])
                s.copy(self.qsc[:, qcol + 1:qcol + 2], self.PSF[2][:, 1:2]).then_inc(SE.h, 1)
            se_b = SE.inc()
            self.psf_free[2] = (SE, se_b)

            @em.on("vector")
            def _(v, w=se_b, qcol=qcol):
                v.wait_ge(SE.h, w)
                v.tensor_scalar(self.qsc[:, qcol + 2:qcol + 3], self.qsc[:, qcol + 1:qcol + 2],
                                -1.0, None, OP.mult).then_inc(VE.h, 1)
            VE.inc()

        if stages >= 3:
            scale_reduce(0, 0)
            rdy_up = (VE, VE.n)
            scale_reduce(64, 4)
            rdy_gate = (VE, VE.n)
            scale_reduce(128, 8)
            rdy_down = (VE, VE.n)
            for i in range(64):
                self.bg.append(lambda i=i: self.prep_tile(
                    I["wup"][i * 128:(i + 1) * 128, :], D["wupT_s"], i * 128, DE,
                    quant=(self.qsc[:, 1:2], self.qsc[:, 2:3], rdy_up)))
            for i in range(64):
                self.bg.append(lambda i=i: self.prep_tile(
                    I["wgate"][i * 128:(i + 1) * 128, :], D["wgateT_s"], i * 128, DE,
                    quant=(self.qsc[:, 5:6], self.qsc[:, 6:7], rdy_gate)))
            for i in range(64):
                rt, qc = i // 4, i % 4
                self.bg.append(lambda rt=rt, qc=qc: self.prep_tile(
                    I["wdown"][rt * 128:(rt + 1) * 128, qc * 2048:(qc + 1) * 2048],
                    D["wdownT_s"], qc * 2048 * DM + rt * 128, DM,
                    quant=(self.qsc[:, 9:10], self.qsc[:, 10:11], rdy_down)))

        if stages < 2:
            self.barrier(2)
            return
        t = lambda nm, sh, dt: e(nc.sbuf_tensor(nm, sh, dt))
        self.rowb = t("rowb", [1, 3 * T], F32)
        self.p_ch2 = t("p_ch2", [33, T], F32)
        xcr = t("xcr", [128, G * T], F32)
        ln1Tr = t("ln1Tr", [128, G * T], BF16)
        xchi = t("xchi", [128, 2 * T], BF16)
        xclo = t("xclo", [128, 2 * T], BF16)
        siluz = t("siluz", [128, G * T], F32)
        winz_bb = t("winz_bb", [128, G * T], BF16)
        dtrep = t("dtrep", [128, 2 * T], F32)
        dtbrep = t("dtbrep", [128, 2 * T], F32)
        crep = t("crep", [128, 2 * T], F32)
        a_bb = t("a_bb", [128, 2 * T], F32)
        b_bb = t("b_bb", [128, 2 * T], F32)
        h_bb = t("h_bb", [128, 2 * T], F32)
        w_bb = t("w_bb", [128, 2 * T], F32R)
        t2 = t("t2buf", [128, T], F32)
        ymix = t("ymixb", [128, G * T], BF16)

        cc_p = self.cc_p
        rep_ready = [0, 0]
        rep_users_ve = [0, 0]
        rep_users_se = [0, 0]
        rep_users_ge = [0, 0]
        pch2_free = 0
        a_free = [0, 0]
        b_free = [0, 0]
        h_free = [0, 0]
        w_free = [0, 0]
        zev_done = [0] * G
        zmm = 0
        xcr_users = 0
        xhilo_users = [0, 0]
        ln1r_users = 0
        ymix_spilled = 0

        def emit_reps(k):
            par = k % 2
            sl = slice(par * T, (par + 1) * T)

            @em.on("scalar")
            def _(eng, k=k, w=cc_p, w2=pch2_free):
                eng.wait_ge(self.s_ccp.h, w)
                eng.wait_ge(VE.h, w2)
                eng.dma_start(self.p_ch2[0:33, :],
                              bass.AP(D["p_out"], k * T, [[S, 33], [1, T]])).then_inc(self.s_pch2.h, 16)
            lp = self.s_pch2.inc(16)

            @em.on("vector")
            def _(v, lp=lp):
                v.wait_ge(self.s_pch2.h, lp)
                v.tensor_scalar(self.rowb[0:1, 0:T], self.p_ch2[32:33, :],
                                self.dtm_sb[0:1, 0:1], None, OP.add)
                v.tensor_scalar(self.rowb[0:1, T:2 * T], self.rowb[0:1, 0:T], -1.0, None, OP.mult)
                v.tensor_tensor(self.rowb[0:1, T:2 * T], self.rowb[0:1, 0:T],
                                self.rowb[0:1, T:2 * T], OP.max)
                v.tensor_scalar(self.rowb[0:1, 2 * T:3 * T], self.rowb[0:1, 0:T],
                                0.0, None, OP.max).then_inc(VE.h, 1)
            ve_u = VE.inc()

            @em.on("scalar")
            def _(s, w=ve_u):
                s.wait_ge(VE.h, w)
                s.activation(self.rowb[0:1, T:2 * T], self.rowb[0:1, T:2 * T], AF.Exp,
                             bias=0.0, scale=-1.0)
                s.activation(self.rowb[0:1, T:2 * T], self.rowb[0:1, T:2 * T], AF.Ln,
                             bias=1.0, scale=1.0).then_inc(SE.h, 1)
            se_sp = SE.inc()

            @em.on("vector")
            def _(v, w=se_sp):
                v.wait_ge(SE.h, w)
                v.tensor_tensor(self.rowb[0:1, 0:T], self.rowb[0:1, 2 * T:3 * T],
                                self.rowb[0:1, T:2 * T], OP.add).then_inc(VE.h, 1)
            ve_dt = VE.inc()

            @em.on("scalar")
            def _(eng, k=k, w=ve_dt):
                eng.wait_ge(VE.h, w)
                eng.dma_start(D["rep_sp"].ap()[k, 16:17, :], self.rowb[0:1, 0:T]).then_inc(self.s_repout.h, 16)
            lr1 = self.s_repout.inc(16)

            sdt = self.s_dtrep[par]

            @em.on("scalar")
            def _(eng, k=k, par=par, w=lr1, wv=rep_users_ve[par], ws=rep_users_se[par],
                  wg=rep_users_ge[par], sdt=sdt):
                eng.wait_ge(self.s_repout.h, w)
                eng.wait_ge(VE.h, wv)
                eng.wait_ge(SE.h, ws)
                eng.wait_ge(GE.h, wg)
                eng.dma_start(dtrep[:, par * T:(par + 1) * T],
                              bass.AP(D["rep_sp"], (k * 17 + 16) * T, [[0, 128], [1, T]])
                              ).then_inc(sdt.h, 16)
            lr2 = sdt.inc(16)

            @em.on("vector")
            def _(v, w=lr2, par=par, sdt=sdt):
                v.wait_ge(sdt.h, w)
                v.tensor_tensor(self.p_ch2[0:16, :], self.p_ch2[0:16, :],
                                dtrep[0:16, par * T:(par + 1) * T], OP.mult).then_inc(VE.h, 1)
            ve_dtb = VE.inc()

            @em.on("scalar")
            def _(eng, k=k, w=ve_dtb):
                eng.wait_ge(VE.h, w)
                eng.dma_start(D["rep_sp"].ap()[k, 0:16, :], self.p_ch2[0:16, :]).then_inc(self.s_repout.h, 16)
            lr3 = self.s_repout.inc(16)

            sbc = self.s_dtbc[par]

            @em.on("scalar")
            def _(eng, k=k, par=par, w=lr3, sbc=sbc):
                eng.wait_ge(self.s_repout.h, w)
                eng.dma_start(dtbrep[:, par * T:(par + 1) * T],
                              bass.AP(D["rep_sp"], k * 17 * T, [[0, 8], [T, 16], [1, T]])
                              ).then_inc(sbc.h, 16)
                eng.dma_start(crep[:, par * T:(par + 1) * T],
                              bass.AP(D["p_out"], 16 * S + k * T, [[0, 8], [S, 16], [1, T]])
                              ).then_inc(sbc.h, 16)
            rep_ready[par] = sbc.inc(32)
            return ve_dtb

        pch2_busy = emit_reps(0)

        for k in range(self.nch):
            par = k % 2
            slp = slice(par * T, (par + 1) * T)
            self.drain_bg(12)

            @em.on("sync")
            def _(eng, k=k, w=xcr_users):
                eng.wait_ge(VE.h, w)
                eng.dma_start(xcr[:], D["xconv_sp"].ap()[k, :, :]).then_inc(self.s_xcr.h, 16)
            lxc = self.s_xcr.inc(16)

            @em.on("sync")
            def _(eng, k=k, w=ln1r_users):
                eng.wait_ge(PE.h, w)
                eng.dma_start(ln1Tr[:], D["ln1T_sp"].ap()[k, :, :]).then_inc(self.s_ln1r.h, 16)
            lln = self.s_ln1r.inc(16)


            for qt in range(4):
                @em.on("sync")
                def _(eng, qt=qt, w=zmm):
                    eng.wait_ge(PE.h, w)
                    eng.dma_start(winz_bb[:],
                                  bass.AP(D["winzT_s"], qt * 512, [[DIH, 128], [128 * DIH, 16], [1, 512]])
                                  ).then_inc(self.s_winz.h, 16)
                lwq = self.s_winz.inc(16)

                for cgl in range(4):
                    zg = qt * 4 + cgl
                    pfree = self.psf_free[4]

                    @em.on("tensor")
                    def _(tt, cgl=cgl, w1=lln, w2=lwq, pfree=pfree):
                        tt.wait_ge(self.s_ln1r.h, w1)
                        tt.wait_ge(self.s_winz.h, w2)
                        tt.wait_ge(pfree[0].h, pfree[1])
                        for dmg in range(16):
                            base = dmg * 512 + cgl * 128
                            mm = tt.matmul(self.PSF[4][:, :], winz_bb[:, base:base + 128],
                                           ln1Tr[:, dmg * T:(dmg + 1) * T],
                                           start=(dmg == 0), stop=(dmg == 15))
                        mm.then_inc(PE.h, 1)
                    pe_z = PE.inc()
                    zmm = pe_z

                    @em.on("scalar")
                    def _(s, zg=zg, w=pe_z, w2=zev_done[zg]):
                        s.wait_ge(PE.h, w)
                        s.wait_ge(VE.h, w2)
                        s.activation(siluz[:, zg * T:(zg + 1) * T], self.PSF[4][:, :],
                                     AF.Silu, bias=0.0, scale=1.0).then_inc(SE.h, 1)
                    self.psf_free[4] = (SE, SE.inc())
            SE_Z = SE.n

            if k + 1 < self.nch:
                pch2_free = pch2_busy
                pch2_busy = emit_reps(k + 1)

            for j in range(256):
                q, jj = j // 16, j % 16
                tp = j % 2
                qp = q % 2
                slq = slice(qp * T, (qp + 1) * T)
                slt = slice(tp * T, (tp + 1) * T)
                pf = tp
                pyq = 2 + (q % 2)
                pfree = self.psf_free[pf]

                if jj == 0:
                    @em.on("vector")
                    def _(v, q=q, qp=qp, slq=slq, w=lxc, w2=xhilo_users[qp]):
                        v.wait_ge(self.s_xcr.h, w)
                        v.wait_ge(PE.h, w2)
                        v.tensor_copy(xchi[:, slq], xcr[:, q * T:(q + 1) * T])
                        v.tensor_tensor(xclo[:, slq], xcr[:, q * T:(q + 1) * T],
                                        xchi[:, slq], OP.subtract).then_inc(VE.h, 1)
                    ve_hilo = VE.inc()

                @em.on("tensor")
                def _(tt, jj=jj, pf=pf, slq=slq, w1=ve_hilo, pfree=pfree):
                    tt.wait_ge(VE.h, w1)
                    tt.wait_ge(pfree[0].h, pfree[1])
                    tt.matmul(self.PSF[pf][:, :], self.mrep_sb[:, jj * 128:(jj + 1) * 128],
                              xchi[:, slq], start=True, stop=False)
                    tt.matmul(self.PSF[pf][:, :], self.mrep_sb[:, jj * 128:(jj + 1) * 128],
                              xclo[:, slq], start=False, stop=True).then_inc(PE.h, 1)
                pe_xr = PE.inc()
                xhilo_users[q % 2] = pe_xr

                @em.on("scalar")
                def _(s, j=j, slt=slt, slp=slp, w1=rep_ready[par], w2=a_free[tp], sbc=self.s_dtbc[par], sdt=self.s_dtrep[par], wdt=self.s_dtrep[par].n):
                    s.wait_ge(sdt.h, wdt)
                    s.wait_ge(VE.h, w2)
                    s.activation(a_bb[:, slt], dtrep[:, slp], AF.Exp, bias=0.0,
                                 scale=self.negA[:, j:j + 1]).then_inc(SE.h, 1)
                se_a = SE.inc()
                rep_users_se[par] = se_a

                @em.on("vector")
                def _(v, slt=slt, slp=slp, pf=pf, w1=pe_xr, w2=rep_ready[par], sbc=self.s_dtbc[par]):
                    v.wait_ge(PE.h, w1)
                    v.wait_ge(sbc.h, w2)
                    v.tensor_tensor(b_bb[:, slt], self.PSF[pf][:, :], dtbrep[:, slp],
                                    OP.mult).then_inc(VE.h, 1)
                ve_b = VE.inc()
                self.psf_free[pf] = (VE, ve_b)
                rep_users_ve[par] = ve_b

                @em.on("vector")
                def _(v, j=j, slt=slt, tp=tp, w1=se_a, w2=h_free[tp], base=VE.n):
                    v.wait_ge(SE.h, w1)
                    v.wait_ge(GE.h, w2)
                    v.tensor_tensor_scan(h_bb[:, slt], a_bb[:, slt], b_bb[:, slt],
                                         self.hcar[:, j:j + 1], OP.mult, OP.add).then_inc(VE.h, 1)
                    v.wait_ge(VE.h, base + 1)
                    v.tensor_copy(self.hcar[:, j:j + 1],
                                  h_bb[:, tp * T + T - 1:tp * T + T]).then_inc(VE.h, 1)
                VE.inc()
                ve_scan = VE.inc()
                a_free[tp] = ve_scan
                b_free[tp] = ve_scan

                @em.on("gpsimd")
                def _(g, slt=slt, slp=slp, w1=ve_scan, w2=w_free[tp], sbc=self.s_dtbc[par],
                      w3=rep_ready[par]):
                    g.wait_ge(VE.h, w1)
                    g.wait_ge(PE.h, w2)
                    g.wait_ge(sbc.h, w3)
                    g.tensor_tensor(w_bb[:, slt], h_bb[:, slt], crep[:, slp], OP.mult).then_inc(GE.h, 1)
                ge_w = GE.inc()
                h_free[tp] = ge_w
                rep_users_ge[par] = ge_w

                pyfree = self.psf_free[pyq] if jj == 0 else None

                @em.on("tensor")
                def _(tt, jj=jj, slt=slt, pyq=pyq, w1=ge_w, pyfree=pyfree):
                    tt.wait_ge(GE.h, w1)
                    if pyfree is not None:
                        tt.wait_ge(pyfree[0].h, pyfree[1])
                    tt.matmul(self.PSF[pyq][:, :], self.my_sb[:, jj * 128:(jj + 1) * 128],
                              w_bb[:, slt], start=(jj == 0), stop=(jj == 15)).then_inc(PE.h, 1)
                pe_y = PE.inc()
                w_free[tp] = pe_y

                if jj == 15:
                    @em.on("vector")
                    def _(v, q=q, pyq=pyq, w1=pe_y, w2=SE_Z, w3=ymix_spilled):
                        v.wait_ge(PE.h, w1)
                        v.wait_ge(SE.h, w2)
                        v.wait_ge(self.s_ymixsp.h, w3)
                        v.scalar_tensor_tensor(t2[:], xcr[:, q * T:(q + 1) * T],
                                               self.dvec_sb[:, q:q + 1], self.PSF[pyq][:, :],
                                               OP.mult, OP.add)
                        v.tensor_tensor(ymix[:, q * T:(q + 1) * T], t2[:],
                                        siluz[:, q * T:(q + 1) * T], OP.mult).then_inc(VE.h, 1)
                    ve_g = VE.inc()
                    self.psf_free[pyq] = (VE, ve_g)
                    zev_done[q] = ve_g
            ve_chunk = VE.n
            xcr_users = ve_chunk
            ln1r_users = zmm

            @em.on("gpsimd")
            def _(g, k=k, w=ve_chunk):
                g.wait_ge(VE.h, w)
                g.dma_start(bass.AP(D["ymix_in"], k * DIH * T, [[T, 128], [128 * T, 16], [1, T]]),
                            ymix[:]).then_inc(self.s_ymixsp.h, 16)
            ymix_spilled = self.s_ymixsp.inc(16)

            if self.cc:
                @em.on("gpsimd")
                def _(g, k=k, w=ymix_spilled):
                    g.wait_ge(self.s_ymixsp.h, w)
                    g.collective_compute("AllGather", OP.bypass, replica_groups=RG_PAIRS,
                                         ins=[D["ymix_in"].ap()[k, :, :].opt()],
                                         outs=[D["ymix_out"].ap()[k, :, :].opt()]).then_inc(self.s_ccy.h, 1)
                self.s_ccy.inc()
            else:
                @em.on("gpsimd")
                def _(g, k=k, w=ymix_spilled):
                    g.wait_ge(self.s_ymixsp.h, w)
                    g.dma_start(D["ymix_out"].ap()[k, 0:DIH, :], D["ymix_in"].ap()[k, :, :]).then_inc(self.s_ccy.h, 16)
                    g.dma_start(D["ymix_out"].ap()[k, DIH:DI, :], D["ymix_in"].ap()[k, :, :]).then_inc(self.s_ccy.h, 16)
                self.s_ccy.inc(32)

            if dbg >= 2:
                @em.on("gpsimd")
                def _(g, k=k, w=ve_chunk):
                    g.wait_ge(VE.h, w)
                    g.dma_start(bass.AP(self.DBG["ymix"].tensor, k * T, [[S, 128], [128 * S, 16], [1, T]]),
                                ymix[:]).then_inc(self.s_ymixsp.h, 16)
                ymix_spilled = self.s_ymixsp.inc(16)

        self.drain_bg(10000)
        self.barrier(2)

    def block2(self, e):
        if self.stages < 3:
            em = self.em
            self.barrier_entry(10)

            @em.on("gpsimd")
            def _(g):
                g.wait_ge(self.BAR.h, 10)
            return

        nc, em, I, D = self.nc, self.em, self.I, self.D
        SE, VE, PE, GE = self.SE, self.VE, self.PE, self.GE
        self.barrier_entry(10)
        t = lambda nm, sh, dt: e(nc.sbuf_tensor(nm, sh, dt))
        big42 = t("big42", [128, 32 * 256], BF16)
        outw_bb = t("outw_bb", [128, 32 * 128], BF16)
        outf = t("outf", [128, 4 * DM], F32)
        ln2T = t("ln2T", [128, G * T], BF16)
        ln2c = t("ln2c", [128, DM], BF16)
        wup_bb = t("wup_bb", [128, G * 256], BF16)
        wgt_bb = t("wgt_bb", [128, G * 256], BF16)
        hid = t("hid", [128, 32 * T], BF16)
        dacc = t("dacc", [128, 4 * DM], F32)
        tsil = t("tsil", [128, T], F32)
        xc_sl = t("xc_sl", [128, DM], F32)
        x2w = t("x2w", [128, DM], F32)
        csc = t("csc", [128, 16], F32)

        if self.dbg >= 3:
            @em.on("gpsimd")
            def _(g):
                g.dma_start(self.DBG["wq"][:, :], D["wupT_s"].ap()[0:128, :]).then_inc(self.s_x2sp.h, 16)
            self.s_x2sp.inc(16)

        big42_users = 0
        outw_users = 0
        wupg_users = 0
        hid_users = 0
        outf_users = 0
        outf_users_s = 0
        ln2t_users = 0
        prepb_users = 0
        tsil_users = 0
        dacc_users_v = 0
        dacc_users_st = 0
        xcsl_users = 0
        x2w_users = 0
        statc_users = 0

        for ko in range(min(8, self.nch)):
            for mh in range(2):
                @em.on("sync")
                def _(eng, ko=ko, mh=mh, w=big42_users):
                    eng.wait_ge(PE.h, w)
                    with eng.register("roff%d_%d" % (ko, mh)) as r:
                        eng.reg_load(r, self.choff_sb[0:1, ko:ko + 1])
                        eng.reg_add(r, r, mh * 256)
                        eng.dma_start(big42[:],
                                      bass.AP(D["ymix_out"], r, [[T, 128], [128 * T, 32], [1, 256]])
                                      ).then_inc(self.s_ymagg.h, 16)
                lag = self.s_ymagg.inc(16)
                if self.dbg >= 3 and ko == 7 and mh == 0:
                    @em.on("gpsimd")
                    def _(g, w=lag):
                        g.wait_ge(self.s_ymagg.h, w)
                        g.dma_start(self.DBG["yma"][:, :], big42[:]).then_inc(self.s_x2sp.h, 16)
                    self.s_x2sp.inc(16)

                for dmb in range(16):
                    @em.on("sync")
                    def _(eng, dmb=dmb, w=outw_users):
                        eng.wait_ge(PE.h, w)
                        eng.dma_start(outw_bb[:],
                                      bass.AP(D["outwT_s"], dmb * 128, [[DM, 128], [128 * DM, 32], [1, 128]])
                                      ).then_inc(self.s_outw.h, 16)
                    low = self.s_outw.inc(16)

                    for ml in range(2):
                        m = mh * 2 + ml
                        pq = 2 + m % 2
                        pfree = self.psf_free[pq]

                        @em.on("tensor")
                        def _(tt, dmb=dmb, ml=ml, pq=pq, w1=lag, w2=low, pfree=pfree):
                            tt.wait_ge(self.s_ymagg.h, w1)
                            tt.wait_ge(self.s_outw.h, w2)
                            tt.wait_ge(pfree[0].h, pfree[1])
                            for dg in range(32):
                                mm = tt.matmul(self.PSF[pq][:, 0:128],
                                               big42[:, dg * 256 + ml * 128: dg * 256 + (ml + 1) * 128],
                                               outw_bb[:, dg * 128:(dg + 1) * 128],
                                               start=(dg == 0), stop=(dg == 31))
                            mm.then_inc(PE.h, 1)
                        pe_o = PE.inc()
                        outw_users = pe_o
                        big42_users = pe_o

                        @em.on("scalar")
                        def _(s, m=m, dmb=dmb, pq=pq, w=pe_o, w2=outf_users):
                            s.wait_ge(PE.h, w)
                            s.wait_ge(VE.h, w2)
                            s.copy(outf[:, m * DM + dmb * 128: m * DM + (dmb + 1) * 128],
                                   self.PSF[pq][:, 0:128]).then_inc(SE.h, 1)
                        self.psf_free[pq] = (SE, SE.inc())
            SE_OUT = SE.n

            if self.dbg >= 3:
                for m in range(4):
                    @em.on("gpsimd")
                    def _(g, ko=ko, m=m, w=SE_OUT):
                        g.wait_ge(SE.h, w)
                        g.dma_start(self.DBG["outf"][ko * T + m * 128: ko * T + (m + 1) * 128, :],
                                    outf[:, m * DM:(m + 1) * DM]).then_inc(self.s_x2sp.h, 16)
                    self.s_x2sp.inc(16)

            for m in range(4):
                @em.on("scalar")
                def _(eng, ko=ko, m=m, w=SE_OUT, w2=xcsl_users):
                    eng.wait_ge(SE.h, w)
                    eng.wait_ge(VE.h, w2)
                    eng.dma_start(xc_sl[:], I["xc_in"][ko * T + m * 128: ko * T + (m + 1) * 128, :]
                                  ).then_inc(self.s_xc2.h, 16)
                lx2 = self.s_xc2.inc(16)

                def ln_stats(src_slice, c0, wait_list, sq_wait):
                    @em.on("vector")
                    def _(v, src=src_slice, c0=c0, wl=tuple(wait_list)):
                        for sh, sv in wl:
                            v.wait_ge(sh.h, sv)
                        v.tensor_reduce(csc[:, c0:c0 + 1], src, AXX, OP.add).then_inc(VE.h, 1)
                    ve_s = VE.inc()

                    @em.on("scalar")
                    def _(s, src=src_slice, c0=c0, wl=tuple(wait_list), sqw=sq_wait):
                        for sh, sv in wl:
                            s.wait_ge(sh.h, sv)
                        s.wait_ge(VE.h, sqw)
                        s.activation(dacc[:, 0:DM], src, AF.Square, bias=0.0, scale=1.0,
                                     accum_out=csc[:, c0 + 1:c0 + 2]).then_inc(SE.h, 1)
                    se_s = SE.inc()

                    @em.on("vector")
                    def _(v, c0=c0, w1=ve_s, w2=se_s, base=VE.n):
                        v.wait_ge(SE.h, w2)
                        v.wait_ge(VE.h, w1)
                        v.tensor_scalar(csc[:, c0:c0 + 1], csc[:, c0:c0 + 1], 1.0 / DM, None, OP.mult).then_inc(VE.h, 1)
                        v.wait_ge(VE.h, base + 1)
                        v.tensor_tensor(csc[:, c0 + 3:c0 + 4], csc[:, c0:c0 + 1], csc[:, c0:c0 + 1], OP.mult)
                        v.tensor_scalar(csc[:, c0 + 2:c0 + 3], csc[:, c0 + 1:c0 + 2], 1.0 / DM, None, OP.mult).then_inc(VE.h, 1)
                        v.wait_ge(VE.h, base + 2)
                        v.tensor_tensor(csc[:, c0 + 2:c0 + 3], csc[:, c0 + 2:c0 + 3], csc[:, c0 + 3:c0 + 4],
                                        OP.subtract).then_inc(VE.h, 1)
                        v.wait_ge(VE.h, base + 3)
                        v.tensor_scalar(csc[:, c0 + 2:c0 + 3], csc[:, c0 + 2:c0 + 3], EPS_LN, None,
                                        OP.add).then_inc(VE.h, 1)
                    VE.inc(3)
                    ve_v = VE.inc()

                    @em.on("scalar")
                    def _(s, c0=c0, w=ve_v, base=SE.n):
                        s.wait_ge(VE.h, w)
                        s.activation(csc[:, c0 + 2:c0 + 3], csc[:, c0 + 2:c0 + 3], AF.Ln,
                                     bias=0.0, scale=1.0).then_inc(SE.h, 1)
                        s.wait_ge(SE.h, base + 1)
                        s.activation(csc[:, c0 + 2:c0 + 3], csc[:, c0 + 2:c0 + 3], AF.Exp,
                                     bias=0.0, scale=-0.5).then_inc(SE.h, 1)
                    SE.inc()
                    return SE.inc(), ve_v

                se_r1, ve_v1 = ln_stats(outf[:, m * DM:(m + 1) * DM], 0,
                                        [(SE, SE_OUT)], statc_users)

                @em.on("vector")
                def _(v, m=m, w1=se_r1, w2=lx2, w3=x2w_users):
                    v.wait_ge(SE.h, w1)
                    v.wait_ge(self.s_xc2.h, w2)
                    v.wait_ge(self.s_x2sp.h, w3)
                    v.tensor_scalar(outf[:, m * DM:(m + 1) * DM], outf[:, m * DM:(m + 1) * DM],
                                    csc[:, 0:1], csc[:, 2:3], OP.subtract, OP.mult)
                    v.tensor_tensor(x2w[:], outf[:, m * DM:(m + 1) * DM], xc_sl[:], OP.add).then_inc(VE.h, 1)
                ve_x2 = VE.inc()
                xcsl_users = ve_x2
                if self.dbg >= 3 and ko == 7 and m == 0:
                    @em.on("gpsimd")
                    def _(g, w=ve_x2):
                        g.wait_ge(VE.h, w)
                        g.dma_start(self.DBG["csc"][:, :], csc[:]).then_inc(self.s_x2sp.h, 16)
                        g.dma_start(self.DBG["xcs"][:, :], xc_sl[:]).then_inc(self.s_x2sp.h, 16)
                        g.dma_start(self.DBG["onorm"][:, :], outf[:, 0:DM]).then_inc(self.s_x2sp.h, 16)
                    self.s_x2sp.inc(48)

                @em.on("gpsimd")
                def _(g, ko=ko, m=m, w=ve_x2):
                    g.wait_ge(VE.h, w)
                    g.dma_start(D["x2_sp"].ap()[ko, m * 128:(m + 1) * 128, :], x2w[:]).then_inc(self.s_x2sp.h, 16)
                x2w_users = self.s_x2sp.inc(16)
                if self.dbg >= 3:
                    @em.on("gpsimd")
                    def _(g, ko=ko, m=m, w=ve_x2):
                        g.wait_ge(VE.h, w)
                        g.dma_start(self.DBG["x2"][ko * T + m * 128: ko * T + (m + 1) * 128, :],
                                    x2w[:]).then_inc(self.s_x2sp.h, 16)
                    x2w_users = self.s_x2sp.inc(16)

                se_r2, ve_v2 = ln_stats(x2w[:], 4, [(VE, ve_x2)], statc_users)
                statc_users = ve_v2

                @em.on("vector")
                def _(v, m=m, w1=se_r2, w2=prepb_users):
                    v.wait_ge(SE.h, w1)
                    v.wait_ge(PE.h, w2)
                    v.tensor_scalar(ln2c[:], x2w[:], csc[:, 4:5], csc[:, 6:7],
                                    OP.subtract, OP.mult).then_inc(VE.h, 1)
                ve_n2 = VE.inc()
                statc_users = ve_n2

                for half in range(2):
                    pb = half
                    pfree = self.psb_free[pb]

                    @em.on("tensor")
                    def _(tt, half=half, pb=pb, w=ve_n2, pfree=pfree, w2=ln2t_users):
                        tt.wait_ge(VE.h, w)
                        tt.wait_ge(pfree[0].h, pfree[1])
                        for ci in range(8):
                            dg = half * 8 + ci
                            mm = tt.transpose(self.PSB[pb][:, ci * 128:(ci + 1) * 128],
                                              ln2c[:, dg * 128:(dg + 1) * 128], self.ident_sb[:])
                        mm.then_inc(PE.h, 1)
                    pe_t2 = PE.inc()
                    prepb_users = pe_t2

                    @em.on("scalar")
                    def _(s, half=half, pb=pb, m=m, w=pe_t2, w2=ln2t_users):
                        s.wait_ge(PE.h, w)
                        s.wait_ge(PE.h, w2)
                        s.copy(bass.AP(ln2T, half * 8 * T + m * 128, [[G * T, 128], [T, 8], [1, 128]]),
                               bass.AP(self.PSB[pb], 0, [[1024, 128], [128, 8], [1, 128]])).then_inc(SE.h, 1)
                    self.psb_free[pb] = (SE, SE.inc())
            SE_LN2T = SE.n

            se_eh0_evac = 0
            for eh in range(2):
                for eb in range(16):
                    @em.on("sync")
                    def _(eng, eh=eh, eb=eb, w=wupg_users):
                        eng.wait_ge(PE.h, w)
                        off = eh * 4096 + eb * 256
                        eng.dma_start(wup_bb[:],
                                      bass.AP(D["wupT_s"], off, [[DE, 128], [128 * DE, 16], [1, 256]])
                                      ).then_inc(self.s_wupg.h, 16)
                        eng.dma_start(wgt_bb[:],
                                      bass.AP(D["wgateT_s"], off, [[DE, 128], [128 * DE, 16], [1, 256]])
                                      ).then_inc(self.s_wupg.h, 16)
                    lug = self.s_wupg.inc(32)

                    for egl in range(2):
                        eg_local = eb * 2 + egl
                        pf0 = self.psf_free[0]
                        pf1 = self.psf_free[1]

                        @em.on("tensor")
                        def _(tt, egl=egl, w1=SE_LN2T, w2=lug, pf0=pf0):
                            tt.wait_ge(SE.h, w1)
                            tt.wait_ge(self.s_wupg.h, w2)
                            tt.wait_ge(pf0[0].h, pf0[1])
                            for dmg in range(16):
                                base = dmg * 256 + egl * 128
                                mm = tt.matmul(self.PSF[0][:, :], wup_bb[:, base:base + 128],
                                               ln2T[:, dmg * T:(dmg + 1) * T],
                                               start=(dmg == 0), stop=(dmg == 15))
                            mm.then_inc(PE.h, 1)
                        pe_up = PE.inc()

                        @em.on("tensor")
                        def _(tt, egl=egl, pf1=pf1):
                            tt.wait_ge(pf1[0].h, pf1[1])
                            for dmg in range(16):
                                base = dmg * 256 + egl * 128
                                mm = tt.matmul(self.PSF[1][:, :], wgt_bb[:, base:base + 128],
                                               ln2T[:, dmg * T:(dmg + 1) * T],
                                               start=(dmg == 0), stop=(dmg == 15))
                            mm.then_inc(PE.h, 1)
                        pe_gt = PE.inc()
                        wupg_users = pe_gt
                        ln2t_users = pe_gt

                        @em.on("scalar")
                        def _(s, w=pe_up, w2=tsil_users):
                            s.wait_ge(PE.h, w)
                            s.wait_ge(VE.h, w2)
                            s.activation(tsil[:], self.PSF[0][:, :], AF.Silu, bias=0.0,
                                         scale=self.qsc[:, 0:1]).then_inc(SE.h, 1)
                        se_sil = SE.inc()
                        self.psf_free[0] = (SE, se_sil)

                        @em.on("vector")
                        def _(v, eg=eg_local, w1=pe_gt, w2=se_sil, w3=hid_users):
                            v.wait_ge(PE.h, w1)
                            v.wait_ge(SE.h, w2)
                            v.wait_ge(PE.h, w3)
                            v.scalar_tensor_tensor(hid[:, eg * T:(eg + 1) * T], self.PSF[1][:, :],
                                                   self.qsc[:, 4:5], tsil[:],
                                                   OP.mult, OP.mult).then_inc(VE.h, 1)
                        ve_h = VE.inc()
                        tsil_users = ve_h
                        self.psf_free[1] = (VE, ve_h)
                VE_HID = VE.n

                if eh == 1:
                    se_eh0_evac = SE.n
                for dmb in range(8):
                    @em.on("sync")
                    def _(eng, eh=eh, dmb=dmb, w=big42_users):
                        eng.wait_ge(PE.h, w)
                        eng.dma_start(big42[:],
                                      bass.AP(D["wdownT_s"], eh * 4096 * DM + dmb * 256,
                                              [[DM, 128], [128 * DM, 32], [1, 256]])
                                      ).then_inc(self.s_dwn.h, 16)
                    lds_ = self.s_dwn.inc(16)

                    for m in range(4):
                        pq = 2 + m % 2
                        pfree = self.psf_free[pq]

                        @em.on("tensor")
                        def _(tt, m=m, pq=pq, w1=VE_HID, w2=lds_, pfree=pfree):
                            tt.wait_ge(VE.h, w1)
                            tt.wait_ge(self.s_dwn.h, w2)
                            tt.wait_ge(pfree[0].h, pfree[1])
                            for ecl in range(32):
                                mm = tt.matmul(self.PSF[pq][:, 0:256],
                                               hid[:, ecl * T + m * 128: ecl * T + (m + 1) * 128],
                                               big42[:, ecl * 256:(ecl + 1) * 256],
                                               start=(ecl == 0), stop=(ecl == 31))
                            mm.then_inc(PE.h, 1)
                        pe_d = PE.inc()
                        big42_users = pe_d
                        hid_users = pe_d

                        if eh == 0:
                            @em.on("scalar")
                            def _(s, m=m, dmb=dmb, pq=pq, w=pe_d, w2=dacc_users_v, w3=dacc_users_st):
                                s.wait_ge(PE.h, w)
                                s.wait_ge(VE.h, w2)
                                s.wait_ge(self.s_yout.h, w3)
                                s.copy(dacc[:, m * DM + dmb * 256: m * DM + (dmb + 1) * 256],
                                       self.PSF[pq][:, 0:256]).then_inc(SE.h, 1)
                            self.psf_free[pq] = (SE, SE.inc())
                        else:
                            @em.on("vector")
                            def _(v, m=m, dmb=dmb, pq=pq, w=pe_d, w2=se_eh0_evac):
                                v.wait_ge(PE.h, w)
                                v.wait_ge(SE.h, w2)
                                sl2 = slice(m * DM + dmb * 256, m * DM + (dmb + 1) * 256)
                                v.tensor_tensor(dacc[:, sl2], dacc[:, sl2], self.PSF[pq][:, 0:256],
                                                OP.add).then_inc(VE.h, 1)
                            self.psf_free[pq] = (VE, VE.inc())
            if self.dbg >= 3 and ko == 0:
                @em.on("gpsimd")
                def _(g, w=VE.n):
                    g.wait_ge(VE.h, w)
                    g.dma_start(self.DBG["hid"].rearrange("(a b) -> a b", a=1) if False else
                                bass.AP(self.DBG["hid"].tensor, 0, [[64 * T // 128, 128], [1, 64 * T // 128]]),
                                hid[0:128, 0:256]).then_inc(self.s_x2sp.h, 16)
                self.s_x2sp.inc(16)
            SE_DOWN, VE_DOWN = SE.n, VE.n

            for m in range(4):
                @em.on("vector")
                def _(v, m=m, w1=VE_DOWN, w2=SE_DOWN):
                    v.wait_ge(SE.h, w2)
                    v.tensor_scalar(dacc[:, m * DM:(m + 1) * DM], dacc[:, m * DM:(m + 1) * DM],
                                    self.qsc[:, 8:9], None, OP.mult).then_inc(VE.h, 1)
                ve_sd = VE.inc()

                @em.on("scalar")
                def _(eng, ko=ko, m=m, w=xcsl_users, w2=x2w_users):
                    eng.wait_ge(VE.h, w)
                    eng.wait_ge(self.s_x2sp.h, w2)
                    eng.dma_start(xc_sl[:], D["x2_sp"].ap()[ko, m * 128:(m + 1) * 128, :]
                                  ).then_inc(self.s_xc2.h, 16)
                lxr = self.s_xc2.inc(16)

                @em.on("vector")
                def _(v, m=m, w=ve_sd):
                    v.tensor_reduce(csc[:, 8:9], dacc[:, m * DM:(m + 1) * DM], AXX, OP.add).then_inc(VE.h, 1)
                ve_s3 = VE.inc()

                @em.on("scalar")
                def _(s, m=m, w=ve_sd, w2=statc_users):
                    s.wait_ge(VE.h, w)
                    s.activation(outf[:, 0:DM], dacc[:, m * DM:(m + 1) * DM],
                                 AF.Square, bias=0.0, scale=1.0,
                                 accum_out=csc[:, 9:10]).then_inc(SE.h, 1)
                se_s3 = SE.inc()

                @em.on("vector")
                def _(v, w1=ve_s3, w2=se_s3, base=VE.n):
                    v.wait_ge(SE.h, w2)
                    v.wait_ge(VE.h, w1)
                    v.tensor_scalar(csc[:, 8:9], csc[:, 8:9], 1.0 / DM, None, OP.mult).then_inc(VE.h, 1)
                    v.wait_ge(VE.h, base + 1)
                    v.tensor_tensor(csc[:, 11:12], csc[:, 8:9], csc[:, 8:9], OP.mult)
                    v.tensor_scalar(csc[:, 10:11], csc[:, 9:10], 1.0 / DM, None, OP.mult).then_inc(VE.h, 1)
                    v.wait_ge(VE.h, base + 2)
                    v.tensor_tensor(csc[:, 10:11], csc[:, 10:11], csc[:, 11:12], OP.subtract).then_inc(VE.h, 1)
                    v.wait_ge(VE.h, base + 3)
                    v.tensor_scalar(csc[:, 10:11], csc[:, 10:11], EPS_LN, None, OP.add).then_inc(VE.h, 1)
                VE.inc(3)
                ve_v3 = VE.inc()

                @em.on("scalar")
                def _(s, w=ve_v3, base=SE.n):
                    s.wait_ge(VE.h, w)
                    s.activation(csc[:, 10:11], csc[:, 10:11], AF.Ln, bias=0.0, scale=1.0).then_inc(SE.h, 1)
                    s.wait_ge(SE.h, base + 1)
                    s.activation(csc[:, 10:11], csc[:, 10:11], AF.Exp, bias=0.0, scale=-0.5).then_inc(SE.h, 1)
                SE.inc()
                se_r3 = SE.inc()

                @em.on("vector")
                def _(v, m=m, w1=se_r3, w2=lxr):
                    v.wait_ge(SE.h, w1)
                    v.wait_ge(self.s_xc2.h, w2)
                    v.tensor_scalar(dacc[:, m * DM:(m + 1) * DM], dacc[:, m * DM:(m + 1) * DM],
                                    csc[:, 8:9], csc[:, 10:11], OP.subtract, OP.mult)
                    v.tensor_tensor(dacc[:, m * DM:(m + 1) * DM], dacc[:, m * DM:(m + 1) * DM],
                                    xc_sl[:], OP.add).then_inc(VE.h, 1)
                ve_fin = VE.inc()
                dacc_users_v = ve_fin
                xcsl_users = ve_fin
                statc_users = ve_fin

                @em.on("gpsimd")
                def _(g, ko=ko, m=m, w=ve_fin):
                    g.wait_ge(VE.h, w)
                    g.dma_start(self.y_out[ko * T + m * 128: ko * T + (m + 1) * 128, :],
                                dacc[:, m * DM:(m + 1) * DM]).then_inc(self.s_yout.h, 16)
                dacc_users_st = self.s_yout.inc(16)
            outf_users = VE.n

        @em.on("gpsimd")
        def _(g, a=self.s_yout.n, b=self.s_x2sp.n):
            g.wait_ge(self.s_yout.h, a)
            if b:
                g.wait_ge(self.s_x2sp.h, b)


def _mk_masks():
    mr = np.zeros((16, 128, 128), dtype=np.float32)
    my = np.zeros((16, 128, 128), dtype=np.float32)
    for jj in range(16):
        for m in range(128):
            mr[jj, 8 * jj + m // 16, m] = 1.0
        for p in range(128):
            my[jj, p, 8 * jj + p // 16] = 1.0
    bf = ml_dtypes.bfloat16
    return (np.ascontiguousarray(mr.transpose(1, 0, 2).reshape(128, 16 * 128)).astype(bf),
            np.ascontiguousarray(my.transpose(1, 0, 2).reshape(128, 16 * 128)))


def _pg(a):
    a = np.asarray(a, dtype=np.float32)
    if a.ndim == 1:
        a = a.reshape(-1, 1)
    return np.ascontiguousarray(a.reshape(16, 128, -1).transpose(1, 0, 2).reshape(128, -1))


def _prep_inputs(L0_attn_x, L0_attn_W_in, L0_attn_conv_w, L0_attn_conv_b,
                 L0_attn_W_x, L0_attn_A_log, L0_attn_dt_bias, L0_attn_out_W,
                 L0_attn_D, L0_ffn_W_up, L0_ffn_W_gate, L0_ffn_W_down):
    bf = ml_dtypes.bfloat16
    mr, my = _mk_masks()
    ident = np.ascontiguousarray(np.eye(128, dtype=np.float32)).astype(bf)
    f32 = lambda a: np.ascontiguousarray(np.asarray(a, dtype=np.float32))

    maps = []
    for c in range(8):
        b, h = c // 2, c % 2
        sl = slice(h * DIH, (h + 1) * DIH)
        A_half = np.asarray(L0_attn_A_log, dtype=np.float32)[sl]
        apair = np.ascontiguousarray(A_half.reshape(256, 8, 16).transpose(1, 2, 0).reshape(128, 256))
        xb = f32(L0_attn_x[b])
        maps.append({
            "x_in": xb,
            "xc_in": np.ascontiguousarray(xb[h * SH:(h + 1) * SH]),
            "winx": f32(L0_attn_W_in[sl]),
            "winz": f32(L0_attn_W_in[DI + h * DIH: DI + (h + 1) * DIH]),
            "convw": _pg(np.asarray(L0_attn_conv_w, dtype=np.float32)[sl, 0, :]),
            "convb": _pg(L0_attn_conv_b[sl]),
            "wxt": np.ascontiguousarray(np.asarray(L0_attn_W_x, dtype=np.float32)[:, sl].T),
            "apair": apair,
            "dtb_in": np.ascontiguousarray(np.asarray(L0_attn_dt_bias, dtype=np.float32).reshape(32, 128).T),
            "dvec": _pg(L0_attn_D[sl]),
            "outw": f32(L0_attn_out_W),
            "wup": f32(L0_ffn_W_up),
            "wgate": f32(L0_ffn_W_gate),
            "wdown": f32(L0_ffn_W_down),
            "identb": ident,
            "maskrep": mr,
            "masky": my,
            "ones_col": np.ones((128, 1), dtype=np.float32),
            "ones_row": np.ones((1, 128), dtype=np.float32),
            "choff": np.arange(h * 8, h * 8 + 8, dtype=np.uint64).reshape(1, 8).astype(np.uint64) * np.uint64(DI * T) if False else (np.arange(h * 8, h * 8 + 8, dtype=np.uint32) * np.uint32(DI * T)).reshape(1, 8),
        })
    return maps


_NC_CACHE = {}


def get_nc(stages=3, dbg=0, nch=NCH, cc=True):
    key = (stages, dbg, nch, cc)
    if key not in _NC_CACHE:
        _NC_CACHE[key] = Bld(stages, dbg, nch, cc).build()
    return _NC_CACHE[key]


def kernel(**inputs):
    maps = _prep_inputs(**inputs)
    nc = get_nc(3, 0)
    res = run_bass_kernel_spmd(nc, maps, core_ids=list(range(8)))
    out = np.empty((B, S, DM), dtype=np.float32)
    for c in range(8):
        b, h = c // 2, c % 2
        out[b, h * SH:(h + 1) * SH, :] = res.results[c]["y_out"]
    return out
